# version 27
# speedup vs baseline: 1.1338x; 1.0420x over previous
from contextlib import ExitStack
import numpy as np
import concourse.bass as bass
import concourse.mybir as mybir

F32 = mybir.dt.float32
BF16 = mybir.dt.bfloat16
AF = mybir.ActivationFunctionType
ALU = mybir.AluOpType
AX = mybir.AxisListType

COMPUTE_ENGINES = ("vector", "scalar", "tensor", "gpsimd")
DMA_QUEUES = ("sync", "gpsimd", "scalar")
CHAINED = {"vector": True, "scalar": True, "gpsimd": True, "tensor": False}


class Chan:
    def __init__(self, key):
        self.key = key
        self.n = 0


class Prog:
    def __init__(self):
        self.items = {e: [] for e in ("sync", "gpsimd", "scalar", "vector", "tensor")}
        self.cnt = {}
        for e in COMPUTE_ENGINES:
            self.cnt["c_" + e] = 0
        self.chans = {}
        self.waited = {e: {} for e in self.items}
        self.n_instr = 0

    def chan(self, name):
        key = "h_" + name
        assert key not in self.chans
        c = Chan(key)
        self.chans[key] = c
        self.cnt[key] = 0
        return c

    def _wait(self, eng, tok):
        if tok is None:
            return
        key, val = tok
        if val <= 0:
            return
        if self.waited[eng].get(key, -1) >= val:
            return
        self.items[eng].append(("wait", key, val))
        self.waited[eng][key] = val
        self.n_instr += 1

    def op(self, eng, name, *a, waits=(), inc=True, **kw):
        assert eng in COMPUTE_ENGINES, eng
        ck = "c_" + eng
        if CHAINED[eng] and self.cnt[ck] > 0:
            self._wait(eng, (ck, self.cnt[ck]))
        for t in waits:
            self._wait(eng, t)
        if inc:
            self.cnt[ck] += 1
        self.items[eng].append(("op", name, a, kw, bool(inc)))
        self.n_instr += 1
        return (ck, self.cnt[ck])

    def dma(self, q, out, in_, chan, waits=(), **kw):
        assert q in DMA_QUEUES, q
        for t in waits:
            self._wait(q, t)
        self.chans[chan.key]
        chan.n += 16
        self.cnt[chan.key] = chan.n
        self.items[q].append(("op", "dma_start", (), dict(out=out, in_=in_, **kw), chan.key))
        self.n_instr += 1
        return (chan.key, chan.n)

    def raw(self, eng, fn, waits=(), chan=None, n_dmas=0):
        for t in waits:
            self._wait(eng, t)
        key = None
        if chan is not None:
            chan.n += 16 * n_dmas
            self.cnt[chan.key] = chan.n
            key = chan.key
        self.items[eng].append(("raw", fn, key))
        self.n_instr += 1
        return (key, chan.n) if chan is not None else None

    def idma(self, chan, waits=(), **kw):
        q = "gpsimd"
        for t in waits:
            self._wait(q, t)
        self.chans[chan.key]
        chan.n += 16
        self.cnt[chan.key] = chan.n
        self.items[q].append(("op", "indirect_dma_start", (), kw, chan.key))
        self.n_instr += 1
        return (chan.key, chan.n)

    def tok(self, eng):
        key = eng if eng.startswith(("c_", "d_")) else "c_" + eng
        return (key, self.cnt[key])

    def finish(self):
        for key, val in sorted(self.cnt.items()):
            if val > 0:
                self._wait("sync", (key, val))

    def emit(self, nc, es):
        keys = {k for k, v in self.cnt.items() if v > 0}
        for e, w in self.waited.items():
            keys.update(w)
        sems = {k: es.enter_context(nc.semaphore(k)) for k in sorted(keys)}
        block = es.enter_context(nc.Block())

        def run_engine(raw, eng):
            with nc.allow_low_precision(reason="intentional bf16 staging"):
                for it in self.items[eng]:
                    if it[0] == "wait":
                        _, key, val = it
                        raw.wait_ge(sems[key], val)
                    elif it[0] == "raw":
                        _, fn, key = it
                        fn(raw, sems[key] if key else None)
                    else:
                        _, name, a, kw, inc = it
                        inst = getattr(raw, name)(*a, **kw)
                        if name in ("dma_start", "indirect_dma_start"):
                            inst.then_inc(sems[inc], 16)
                        elif inc:
                            inst.then_inc(sems["c_" + eng], 1)

        for eng in ("sync", "gpsimd", "scalar", "vector", "tensor"):
            if not self.items[eng]:
                continue
            getattr(block, eng)(lambda raw, e=eng: run_engine(raw, e))
        return sems


_KLIB_DOC = """klib: emitters for the macaron_d2 megakernel (per-core program).

Per core: 4 batch elems, T=2048 tokens each, D=1024.
 - residual res_sb [128, 16, 1024] f32 resident (token = tp*128 + p)
 - transposed activations xT [128, 8, 2048] bf16 (dim = dp*128 + p)
 - weights host-pre-transposed to [in_dim, out_dim] layouts (load as matmul rhs)
"""
import numpy as np
import math
from contextlib import ExitStack
import concourse.bass as bass
import concourse.mybir as mybir

T = 2048
D = 1024
I_FFN = 2816
H_MOE = 4096
NE_MOE = 8
NH = 16
DH = 64
NTP = T // 128
NDP = D // 128
SHIFT = 15.0


def bucket_of(d):
    d = np.asarray(d, dtype=np.int64)
    nbk = 16
    rel = (d > 0).astype(np.int64) * nbk
    rp = np.abs(d)
    me = 8
    small = rp < me
    rl = me + (np.log(np.maximum(rp, 1).astype(np.float32) / me)
               / math.log(128 / me) * (nbk - me)).astype(np.int32)
    rl = np.minimum(rl, nbk - 1)
    return (rel + np.where(small, rp, rl)).astype(np.int64)


def host_consts():
    import ml_dtypes
    c = {}
    c["identity128"] = np.eye(128, dtype=np.float32).astype(ml_dtypes.bfloat16)
    c["ones128"] = np.ones((128, 128), dtype=np.float32).astype(ml_dtypes.bfloat16)
    BO = np.zeros((32, 512), dtype=np.float32)
    tt = np.arange(512)
    bk_ = bucket_of(tt - 255)
    for ti in range(128, 512):
        BO[bk_[ti], ti] += 8.0
        BO[15, ti] -= 8.0
    c["bias_onehot"] = BO
    A8 = np.zeros((32, 128), dtype=np.float32)
    A8[31, :] = 8.0
    A8[15, :] = -8.0
    c["auxD8"] = A8
    tt2 = np.arange(128)
    c["sut128"] = (tt2[:, None] < tt2[None, :]).astype(np.float32).astype(ml_dtypes.bfloat16)
    c["tokids"] = (tt2[:, None] + np.arange(16)[None, :] * 128).astype(np.int32)
    c["tokidf"] = c["tokids"].astype(np.float32)
    c["rcw1"] = (tt2[:, None] * 2 + np.array([0, 1])[None, :]).astype(np.float32)
    c["rcw2"] = (tt2[:, None] + np.arange(4)[None, :] * 128).astype(np.float32)
    c["dtpos"] = (np.arange(19, dtype=np.float32) * 384).reshape(1, 19)
    return c


class K:
    def __init__(self, nc, es, p=None):
        self.nc = nc
        self.es = es
        self.p = p or Prog()
        self.sb = {}
        self.dram = {}
        self.chans = {}
        self.aux = host_consts()
        self.exempt = set()
        self.uid = 0

    def u(self, base):
        self.uid += 1
        return f"{base}{self.uid}"

    def sbuf(self, name, shape, dt):
        t = self.es.enter_context(self.nc.sbuf_tensor(name, shape, dt))
        self.sb[name] = t
        return t

    def chan(self, name):
        if name not in self.chans:
            self.chans[name] = self.p.chan(name)
        return self.chans[name]

    def din(self, name, shape, dt=F32):
        t = self.nc.dram_tensor(name, shape, dt, kind="ExternalInput").ap()
        self.dram[name] = t
        return t

    def dout(self, name, shape, dt=F32):
        t = self.nc.dram_tensor(name, shape, dt, kind="ExternalOutput").ap()
        self.dram[name] = t
        return t

    def dscratch(self, name, shape, dt):
        t = self.nc.dram_tensor(name, shape, dt, kind="Internal").ap()
        self.dram[name] = t
        return t

    def join(self):
        p = self.p
        toks = [(kk, v) for kk, v in p.cnt.items()
                if v > 0 and kk not in self.exempt]
        for eng in ("sync", "gpsimd", "scalar", "vector", "tensor"):
            for t in toks:
                if t[0] == "c_" + eng:
                    continue
                p._wait(eng, t)


def sap(tens, offset, dims):
    return bass.AP(tens[:].tensor, offset, dims)


def emit_consts(k: K):
    p = k.p
    ch = k.chan("consts")
    tok = None
    for n in ("identity128", "ones128", "bias_onehot", "auxD8", "sut128", "tokids",
              "tokidf", "rcw1", "rcw2", "dtpos"):
        arr = k.aux[n]
        dt = F32 if arr.dtype == np.float32 else (
            mybir.dt.int32 if arr.dtype == np.int32 else BF16)
        k.sbuf(n, list(arr.shape), dt)
        d = k.din("c_" + n, list(arr.shape), dt)
        tok = p.dma("sync", k.sb[n][:], d[:], ch)
    fc = k.sbuf("fconsts", [128, 2], F32)
    p.op("gpsimd", "memset", fc[:, 0:1], 1e-5)
    p.op("gpsimd", "memset", fc[:, 1:2], -SHIFT)
    k.const_tok = tok
    return tok


def emit_ln_tile(k: K, x, out_bf, sq, st, waits, out_f32=None):
    p = k.p
    p.op("vector", "reduce_sum", out=st[:, 0:1].unsqueeze(2), in_=x.unsqueeze(1),
         axis=AX.X, waits=waits)
    p.op("vector", "tensor_mul", out=sq[:], in0=x, in1=x)
    p.op("vector", "reduce_sum", out=st[:, 1:2].unsqueeze(2), in_=sq[:].unsqueeze(1), axis=AX.X)
    p.op("vector", "tensor_scalar", out=st[:, 2:3], in0=st[:, 0:1], scalar1=1.0 / D,
         scalar2=None, op0=ALU.mult)
    p.op("vector", "tensor_scalar", out=st[:, 3:4], in0=st[:, 1:2], scalar1=1.0 / D,
         scalar2=None, op0=ALU.mult)
    p.op("vector", "tensor_mul", out=st[:, 4:5], in0=st[:, 2:3], in1=st[:, 2:3])
    tv = p.op("vector", "tensor_sub", out=st[:, 5:6], in0=st[:, 3:4], in1=st[:, 4:5])
    p.op("scalar", "activation", out=st[:, 6:7], in_=st[:, 5:6], func=AF.Ln,
         bias=k.sb["fconsts"][:, 0:1], scale=1.0, waits=[tv])
    tr = p.op("scalar", "activation", out=st[:, 7:8], in_=st[:, 6:7], func=AF.Exp, scale=-0.5)
    tn = p.op("vector", "tensor_scalar", out=out_bf, in0=x, scalar1=st[:, 2:3],
              scalar2=st[:, 7:8], op0=ALU.subtract, op1=ALU.mult, waits=[tr])
    if out_f32 is not None:
        tn = p.op("vector", "tensor_scalar", out=out_f32, in0=x, scalar1=st[:, 2:3],
                  scalar2=st[:, 7:8], op0=ALU.subtract, op1=ALU.mult)
    return tn


def emit_transpose(k: K, src_bf, dst_bf, n_tok_pt, n_d_pt, waits, banks):
    p = k.p
    ident = k.sb["identity128"]
    last = {}
    nb = 0
    ev = None
    for dp in range(n_d_pt):
        for g0 in range(0, n_tok_pt, 8):
            gsz = min(8, n_tok_pt - g0)
            bank = banks[nb % 2]
            w = (list(waits) + [k.const_tok]) if nb < 2 else [last[nb - 2]]
            for gi in range(gsz):
                p.op("tensor", "transpose", out=bank[:, gi * 128:(gi + 1) * 128],
                     in_=src_bf[:, g0 + gi, dp * 128:(dp + 1) * 128],
                     identity=ident[:], waits=w if gi == 0 else (), inc=(gi == gsz - 1))
            mt = p.tok("tensor")
            ev = p.op("vector", "tensor_copy",
                      out=dst_bf[:, dp, g0 * 128:(g0 + gsz) * 128],
                      in_=bank[:, 0:gsz * 128], waits=[mt])
            last[nb] = ev
            nb += 1
    return ev


def emit_quant_tile(k: K, x_f32, out_bf, scr, n, waits, eng="vector"):
    p = k.p
    nb = n // 16
    rmax, scale, rsc = scr["q_rmax"], scr["q_scale"], scr["q_rsc"]
    absn, acc4, acc2 = scr["q_absn"], scr["q_acc4"], scr["q_acc2"]
    sgn = scr["q_acc2"]
    w0 = list(waits) + ([scr["_g"]] if "_g" in scr else [])
    trm = p.op("vector", "tensor_reduce", out=rmax[:, 0:nb].unsqueeze(2),
               in_=x_f32.rearrange("p (b k) -> p b k", k=16), axis=AX.X,
               op=ALU.max, apply_absolute_value=True, waits=w0)
    tsc = p.op(eng, "tensor_scalar", out=scale[:, 0:nb], in0=rmax[:, 0:nb],
               scalar1=1.0 / 6.0, scalar2=1e-12, op0=ALU.mult, op1=ALU.max,
               waits=w0 + [trm])
    trc = p.op("vector", "reciprocal", out=rsc[:, 0:nb], in_=scale[:, 0:nb],
               waits=[tsc] + ([scr["_rscw"]] if "_rscw" in scr else []))
    rsc_b = sap(rsc, 0, [[rsc.shape[1], 128], [1, nb], [0, 16]])
    t_abs = p.op("scalar", "activation", out=absn[:, 0:n], in_=x_f32, func=AF.Abs,
                 waits=w0 + [p.tok(eng)])
    geng = eng
    tm0 = p.op(geng, "tensor_tensor",
               out=absn[:, 0:n].rearrange("p (b k) -> p b k", k=16),
               in0=absn[:, 0:n].rearrange("p (b k) -> p b k", k=16), in1=rsc_b,
               op=ALU.mult, waits=[t_abs, trc] + ([scr["_g"]] if "_g" in scr else []))
    scr["_rscw"] = tm0
    if eng == "vector":
        tmp = scr["q_tmp"]
        b1 = float(np.float32(64.0) - np.float32(2.0 ** -10))
        b2 = float(np.float32(128.0) - np.float32(2.0 ** -10))
        p.op(eng, "tensor_scalar", out=acc4[:, 0:n], in0=absn[:, 0:n],
             scalar1=2.0, scalar2=b1, op0=ALU.min, op1=ALU.add, waits=[tm0])
        p.op(eng, "tensor_scalar", out=acc2[:, 0:n], in0=absn[:, 0:n],
             scalar1=2.0, scalar2=4.0, op0=ALU.max, op1=ALU.min)
        p.op(eng, "tensor_scalar", out=acc2[:, 0:n], in0=acc2[:, 0:n],
             scalar1=b2, scalar2=None, op0=ALU.add)
        p.op(eng, "tensor_scalar", out=tmp[:, 0:n], in0=absn[:, 0:n],
             scalar1=5.0, scalar2=2.0, op0=ALU.is_gt, op1=ALU.mult)
        p.op(eng, "scalar_tensor_tensor", out=acc4[:, 0:n], in0=acc4[:, 0:n],
             scalar=64.0, in1=tmp[:, 0:n], op0=ALU.subtract, op1=ALU.add)
        t_comb = p.op(eng, "scalar_tensor_tensor", out=acc4[:, 0:n], in0=acc2[:, 0:n],
                      scalar=130.0, in1=acc4[:, 0:n], op0=ALU.subtract, op1=ALU.add)
    else:
        raise NotImplementedError("quant chain is vector-only")
    scale_b = sap(scale, 0, [[scale.shape[1], 128], [1, nb], [0, 16]])
    tm = p.op(geng, "tensor_tensor",
              out=acc4[:, 0:n].rearrange("p (b k) -> p b k", k=16),
              in0=acc4[:, 0:n].rearrange("p (b k) -> p b k", k=16), in1=scale_b,
              op=ALU.mult, waits=[t_comb])
    ts = p.op("scalar", "sign", out=sgn[:, 0:n], in_=x_f32,
              waits=[t_abs, t_comb])
    tg = p.op(geng, "tensor_tensor", out=out_bf, in0=acc4[:, 0:n],
              in1=sgn[:, 0:n], op=ALU.mult, waits=[ts, tm])
    scr["_g"] = tg
    return tg


def quant_scratch(k: K, es2, pref, n):
    E = es2.enter_context
    nc = k.nc
    return {
        "q_rmax": E(nc.sbuf_tensor(pref + "rmax", [128, n // 16], F32)),
        "q_scale": E(nc.sbuf_tensor(pref + "scale", [128, n // 16], F32)),
        "q_rsc": E(nc.sbuf_tensor(pref + "rsc", [128, n // 16], F32)),
        "q_absn": E(nc.sbuf_tensor(pref + "absn", [128, n], BF16)),
        "q_acc4": E(nc.sbuf_tensor(pref + "acc4", [128, n], BF16)),
        "q_acc2": E(nc.sbuf_tensor(pref + "acc2", [128, n], BF16)),
        "q_tmp": E(nc.sbuf_tensor(pref + "tmp", [128, n], BF16)),
    }


def emit_ln_stats(k: K, res_sb, st, waits):
    p = k.p
    with ExitStack() as es4:
        sq = es4.enter_context(k.nc.sbuf_tensor(k.u("lns_sq"), [128, 1024], F32))
        for tp in range(NTP):
            x = res_sb[:, tp, :]
            p.op("vector", "reduce_sum", out=st[:, tp:tp + 1].unsqueeze(2),
                 in_=x.unsqueeze(1), axis=AX.X, waits=waits if tp == 0 else ())
            p.op("vector", "tensor_mul", out=sq[:], in0=x, in1=x)
            p.op("vector", "reduce_sum", out=st[:, 16 + tp:17 + tp].unsqueeze(2),
                 in_=sq[:].unsqueeze(1), axis=AX.X)
        p.op("vector", "tensor_scalar", out=st[:, 32:48], in0=st[:, 0:16],
             scalar1=1.0 / D, scalar2=None, op0=ALU.mult)
        p.op("vector", "tensor_scalar", out=st[:, 48:64], in0=st[:, 16:32],
             scalar1=1.0 / D, scalar2=None, op0=ALU.mult)
        p.op("vector", "tensor_mul", out=st[:, 64:80], in0=st[:, 32:48], in1=st[:, 32:48])
        tv = p.op("vector", "tensor_sub", out=st[:, 64:80], in0=st[:, 48:64], in1=st[:, 64:80])
        p.op("scalar", "activation", out=st[:, 80:96], in_=st[:, 64:80], func=AF.Ln,
             bias=k.sb["fconsts"][:, 0:1], scale=1.0, waits=[tv])
        tr = p.op("scalar", "activation", out=st[:, 80:96], in_=st[:, 80:96],
                  func=AF.Exp, scale=-0.5)
    return tr


def emit_ln_transpose(k: K, res_sb, es2, waits, store=None):
    nc = k.nc
    p = k.p
    xT = es2.enter_context(nc.sbuf_tensor(k.u("lt_xT"), [128, NDP, T], BF16))
    with ExitStack() as es3:
        E3 = es3.enter_context
        xbt = [E3(nc.sbuf_tensor(k.u("lt_xbt"), [128, 1024], BF16)) for _ in range(2)]
        st = E3(nc.sbuf_tensor(k.u("lt_st"), [128, 96], F32))
        banks = [E3(nc.psum_tensor(k.u("lt_pt"), [128, 1024], BF16)) for _ in range(2)]
        tr = emit_ln_stats(k, res_sb, st, waits)
        ev = {}
        stok = {}
        for tp in range(NTP):
            w = [tr] + ([stok[tp - 2]] if store and tp - 2 in stok else [])
            t = p.op("vector", "tensor_scalar", out=xbt[tp % 2][:],
                     in0=res_sb[:, tp, :], scalar1=st[:, 32 + tp:33 + tp],
                     scalar2=st[:, 80 + tp:81 + tp], op0=ALU.subtract, op1=ALU.mult,
                     waits=w)
            if store is not None:
                sd, sch = store
                stok[tp] = p.dma("sync", sd[tp * 128:(tp + 1) * 128, :],
                                 xbt[tp % 2][:], sch[tp % 2], waits=[t])
            bank = banks[tp % 2]
            for dp in range(NDP):
                w = [t, k.const_tok] + ([ev[tp - 2]] if tp - 2 in ev else [])
                p.op("tensor", "transpose", out=bank[:, dp * 128:(dp + 1) * 128],
                     in_=xbt[tp % 2][:, dp * 128:(dp + 1) * 128],
                     identity=k.sb["identity128"][:],
                     waits=w if dp == 0 else (), inc=(dp == NDP - 1))
            mt = p.tok("tensor")
            ev[tp] = p.op("vector", "tensor_copy",
                          out=xT[:, :, tp * 128:(tp + 1) * 128],
                          in_=bank[:].rearrange("p (a b) -> p a b", b=128),
                          waits=[mt])
        if store is not None:
            p._wait("vector", stok[NTP - 1])
            p._wait("vector", stok[NTP - 2])
    return xT, ev[NTP - 1]


def emit_moe_cast(k: K, w1_d, w2_d, w1b_d, w2b_d, waits=()):
    p = k.p
    ch = k.chan("cast0")
    k.exempt.add(ch.key)
    t = None
    for e in range(NE_MOE):
        t = p.dma("gpsimd", w1b_d[e], w1_d[e], ch,
                  waits=list(waits) if e == 0 else [])
        t = p.dma("gpsimd", w2b_d[e], w2_d[e], ch)
    k.cast_toks = [t]
    return t


def emit_l0ffn_prep(k: K, wg_d, wu_d, wd_d, wgqT_d, wuqT_d, wdqT_d):
    p = k.p
    nc = k.nc
    k.join()
    with ExitStack() as es2:
        E = es2.enter_context
        xt_l = [E(nc.sbuf_tensor(f"wp_in{i}", [128, 2816], F32)) for i in range(2)]
        qb_l = [E(nc.sbuf_tensor(f"wp_q{i}", [128, 2816], BF16)) for i in range(2)]
        trb = E(nc.sbuf_tensor("wp_tr", [128, 22, 128], BF16))
        scr2 = [quant_scratch(k, es2, f"wp{i}_", 2816) for i in range(2)]
        ptr = [E(nc.psum_tensor(f"wp_pt{i}", [128, 1024], BF16)) for i in range(2)]
        chi = k.chan("wprep_in")
        cho = k.chan("wprep_out")
        t_store = None
        tr_mt = {}
        wp_prev_evs = []

        def do_tiles(src_d, dst_d, n_tiles, width):
            nonlocal t_store, wp_prev_evs
            ndp = width // 128
            for it in range(n_tiles):
                par = it % 2
                xt = xt_l[par]
                qb = qb_l[par]
                scr = scr2[par]
                w = ([scr["_g"]] if "_g" in scr else [])
                tin = p.dma("sync", xt[:, 0:width], src_d[it * 128:(it + 1) * 128, :], chi, waits=w)
                tq = emit_quant_tile(k, xt[:, 0:width], qb[:, 0:width], scr, width,
                                     [tin] + ([tr_mt[par]] if par in tr_mt else []),
                                     eng="vector")
                evs = []
                for dp in range(ndp):
                    bank = ptr[dp % 2]
                    tw = ([tq, k.const_tok] + wp_prev_evs[-2:]) if dp < 2 else [evs[dp - 2]]
                    p.op("tensor", "transpose", out=bank[:, 0:128],
                         in_=qb[:, dp * 128:(dp + 1) * 128],
                         identity=k.sb["identity128"][:], waits=tw, inc=True)
                    mt = p.tok("tensor")
                    ev = p.op("vector", "tensor_copy", out=trb[:, dp, :], in_=bank[:, 0:128],
                              waits=[mt] + ([t_store] if t_store and dp == 0 else []))
                    evs.append(ev)
                tr_mt[par] = p.tok("tensor")
                wp_prev_evs = evs[-2:]
                t_store = p.dma("sync",
                                dst_d[:, it * 128:(it + 1) * 128].rearrange("(a p) i -> p a i", p=128),
                                trb[:, 0:ndp, :], cho, waits=[evs[-1]])

        do_tiles(wg_d, wgqT_d, 22, 1024)
        do_tiles(wu_d, wuqT_d, 22, 1024)
        do_tiles(wd_d, wdqT_d, 8, 2816)
        return t_store


def emit_l0ffn(k: K, res_sb, wgqT_d, wuqT_d, wdqT_d):
    p = k.p
    nc = k.nc
    k.join()
    with ExitStack() as es2:
        E = es2.enter_context
        xqT = E(nc.sbuf_tensor(k.u("f_xqT"), [128, NDP, T], BF16))
        scr2 = [quant_scratch(k, es2, k.u("f_q"), 1024) for _ in range(2)]
        st = E(nc.sbuf_tensor(k.u("f_st"), [128, 96], F32))
        with ExitStack() as es3:
            xq = es3.enter_context(nc.sbuf_tensor(k.u("f_xq"), [128, NTP, 1024], BF16))
            xln_l = [es3.enter_context(nc.sbuf_tensor(k.u("f_xln"), [128, 1024], F32))
                     for _ in range(2)]
            tr = emit_ln_stats(k, res_sb, st, ())
            tq_par = [None, None]
            for tp in range(NTP):
                par = tp % 2
                xln = xln_l[par]
                scr = scr2[par]
                t = p.op("vector", "tensor_scalar", out=xln[:], in0=res_sb[:, tp, :],
                         scalar1=st[:, 32 + tp:33 + tp], scalar2=st[:, 80 + tp:81 + tp],
                         op0=ALU.subtract, op1=ALU.mult,
                         waits=[tr] + ([scr["_g"]] if "_g" in scr else []))
                tq_par[par] = emit_quant_tile(k, xln[:], xq[:, tp, :], scr, 1024, [t],
                                              eng="vector")
            with ExitStack() as es4:
                banks = [es4.enter_context(nc.psum_tensor(k.u("f_pt"), [128, 1024], BF16))
                         for _ in range(2)]
                t_tr = emit_transpose(k, xq, xqT, NTP, NDP,
                                      [tq_par[0], tq_par[1]], banks)
        with ExitStack() as es3:
            E3 = es3.enter_context
            wg_c = [E3(nc.sbuf_tensor(k.u("f_wg"), [128, 8, 512], BF16)) for _ in range(2)]
            wu_c = [E3(nc.sbuf_tensor(k.u("f_wu"), [128, 8, 512], BF16)) for _ in range(2)]
            wd_c = [E3(nc.sbuf_tensor(k.u("f_wd"), [128, 22, 256], BF16)) for _ in range(2)]
            qstage = [E3(nc.sbuf_tensor(k.u("f_qs"), [128, 512], BF16)) for _ in range(2)]
            hqT = E3(nc.sbuf_tensor(k.u("f_hqT"), [128, 22, 512], BF16))
            sil_l = [E3(nc.sbuf_tensor(k.u("f_sil"), [128, 512], F32)) for _ in range(2)]
            hf32_l = [E3(nc.sbuf_tensor(k.u("f_hf32"), [128, 512], F32)) for _ in range(2)]
            pga_l = [E3(nc.psum_tensor(k.u("f_pga"), [128, 512], F32)) for _ in range(2)]
            pgb_l = [E3(nc.psum_tensor(k.u("f_pgb"), [128, 512], F32)) for _ in range(2)]
            pdn = [E3(nc.psum_tensor(k.u("f_pdn"), [128, 512], F32)) for _ in range(2)]
            ptr = [E3(nc.psum_tensor(k.u("f_ptr"), [128, 1024], BF16)) for _ in range(2)]
            chw = [k.chan("f_w0"), k.chan("f_w1")]
            chd_l = [k.chan("f_d0"), k.chan("f_d1")]
            IC = [(0, 512), (512, 512), (1024, 512), (1536, 512), (2048, 512), (2560, 256)]
            wfree = {}
            t_dn = []
            t_wd_used = {}
            t_hq = None
            pga_hist = []
            pgb_hist = []
            tr_ev = {}

            qs_mt = {}
            prev_evs = []

            def flush_transposes(pend):
                nonlocal prev_evs
                if pend is None:
                    return None
                qs, tpl, i0, iw, tq = pend
                evs = []
                for j in range(iw // 128):
                    dp = i0 // 128 + j
                    bank = ptr[j % 2]
                    tw = ([tq, k.const_tok] + prev_evs[-2:]) if j < 2 else [evs[j - 2]]
                    p.op("tensor", "transpose", out=bank[:, 0:128],
                         in_=qs[:, j * 128:(j + 1) * 128],
                         identity=k.sb["identity128"][:], waits=tw, inc=True)
                    mt = p.tok("tensor")
                    ev = p.op("vector", "tensor_copy",
                              out=hqT[:, dp, tpl * 128:(tpl + 1) * 128],
                              in_=bank[:, 0:128], waits=[mt])
                    evs.append(ev)
                qs_mt[qstage.index(qs)] = p.tok("tensor")
                prev_evs = evs[-2:]
                return evs[-1]

            for tokb in range(4):
                pend = None
                t_tr_last = None
                for ici, (i0, iw) in enumerate(IC):
                    bi = ici % 2
                    gidx = tokb * 6 + ici
                    wl = [wfree[gidx - 2]] if gidx - 2 in wfree else [t_tr]
                    p.dma("sync", wg_c[bi][:, :, 0:iw],
                          wgqT_d[:, i0:i0 + iw].rearrange("(a p) i -> p a i", p=128),
                          chw[bi], waits=wl)
                    tu = p.dma("sync", wu_c[bi][:, :, 0:iw],
                               wuqT_d[:, i0:i0 + iw].rearrange("(a p) i -> p a i", p=128),
                               chw[bi], waits=[])
                    for tpl in range(4):
                        tp = tokb * 4 + tpl
                        gti = len(pga_hist)
                        pga = pga_l[gti % 2]
                        pgb = pgb_l[gti % 2]
                        w0 = [tu, t_tr] + ([pga_hist[-2]] if len(pga_hist) >= 2 else [])
                        for ec in range(8):
                            p.op("tensor", "matmul", out=pga[:, 0:iw],
                                 lhsT=xqT[:, ec, tp * 128:(tp + 1) * 128],
                                 rhs=wg_c[bi][:, ec, 0:iw], start=(ec == 0), stop=(ec == 7),
                                 waits=w0 if ec == 0 else (), inc=(ec == 7))
                        t_g = p.tok("tensor")
                        w0 = [pgb_hist[-2]] if len(pgb_hist) >= 2 else []
                        for ec in range(8):
                            p.op("tensor", "matmul", out=pgb[:, 0:iw],
                                 lhsT=xqT[:, ec, tp * 128:(tp + 1) * 128],
                                 rhs=wu_c[bi][:, ec, 0:iw], start=(ec == 0), stop=(ec == 7),
                                 waits=w0 if ec == 0 else (), inc=(ec == 7))
                        t_u = p.tok("tensor")
                        t_tr_last = flush_transposes(pend) or t_tr_last
                        par = (gidx * 4 + tpl) % 2
                        sil = sil_l[par]
                        hf32 = hf32_l[par]
                        scr = scr2[par]
                        t_sil = p.op("scalar", "activation", out=sil[:, 0:iw],
                                     in_=pga[:, 0:iw], func=AF.Silu,
                                     waits=[t_g] + ([pgb_hist[-2]] if len(pgb_hist) >= 2 else []))
                        pga_hist.append(t_sil)
                        t_h = p.op("vector", "tensor_tensor", out=hf32[:, 0:iw],
                                   in0=pgb[:, 0:iw], in1=sil[:, 0:iw], op=ALU.mult,
                                   waits=[t_u, t_sil] + ([scr["_g"]] if "_g" in scr else []))
                        pgb_hist.append(t_h)
                        qs = qstage[par]
                        t_hq = emit_quant_tile(k, hf32[:, 0:iw], qs[:, 0:iw], scr, iw,
                                               [t_h] + ([qs_mt[par]] if par in qs_mt else []),
                                               eng="vector")
                        pend = (qs, tpl, i0, iw, t_hq)
                        wfree[gidx] = t_hq
                t_tr_last = flush_transposes(pend) or t_tr_last
                t_trh = t_tr_last
                for oc in range(4):
                    oci = tokb * 4 + oc
                    wdb = wd_c[oci % 2]
                    td = p.dma("sync", wdb[:],
                               wdqT_d[:, oc * 256:(oc + 1) * 256].rearrange("(a p) i -> p a i", p=128),
                               chd_l[oci % 2], waits=[t_wd_used[oci - 2]] if (oci - 2) in t_wd_used else [t_tr])
                    for tpl in range(4):
                        tp = tokb * 4 + tpl
                        pb = pdn[tpl % 2]
                        w0 = [td, t_trh] + ([t_dn[-2]] if len(t_dn) >= 2 else [])
                        for cc in range(22):
                            p.op("tensor", "matmul", out=pb[:, 0:256],
                                 lhsT=hqT[:, cc, tpl * 128:(tpl + 1) * 128],
                                 rhs=wdb[:, cc, :], start=(cc == 0), stop=(cc == 21),
                                 waits=w0 if cc == 0 else (), inc=(cc == 21))
                        tmm = p.tok("tensor")
                        tev = p.op("vector", "scalar_tensor_tensor",
                                   out=res_sb[:, tp, oc * 256:(oc + 1) * 256],
                                   in0=pb[:, 0:256], scalar=0.5,
                                   in1=res_sb[:, tp, oc * 256:(oc + 1) * 256],
                                   op0=ALU.mult, op1=ALU.add, waits=[tmm])
                        t_dn.append(tev)
                    t_wd_used[oci] = p.tok("tensor")
    return t_dn[-1]


def emit_l0attn(k: K, res_sb, wqT_d, wkT_d, wvT_d, bq_d, bk_d, bv_d):
    p = k.p
    nc = k.nc
    k.join()
    with ExitStack() as es2:
        E = es2.enter_context
        bqc = E(nc.sbuf_tensor(k.u("a_bqc"), [128, 16], F32))
        bvb = E(nc.sbuf_tensor(k.u("a_bvb"), [128, 1024], BF16))
        chb = k.chan("a_b")
        t_bq = p.dma("sync", bqc[:, 0:8], bq_d[:], chb)
        t_bk = p.dma("sync", bqc[:, 8:16], bk_d[:], chb)
        t_bvr = p.dma("gpsimd", bvb[0:1, :], bv_d[None, :], k.chan("a_bv"))
        QT = E(nc.sbuf_tensor(k.u("a_QT"), [128, NDP, T], BF16))
        KT = E(nc.sbuf_tensor(k.u("a_KT"), [128, NDP, T], BF16))
        V = E(nc.sbuf_tensor(k.u("a_V"), [128, NTP, 1024], BF16))
        with ExitStack() as es3:
            E3 = es3.enter_context
            xT, t_tr = emit_ln_transpose(k, res_sb, es3, ())
            pq = [E3(nc.psum_tensor(k.u("a_pq"), [128, 512], F32)) for _ in range(2)]
            tbv = []
            for c in range(2):
                tmb = p.op("tensor", "matmul", out=pq[c][:, 0:512],
                           lhsT=k.sb["ones128"][0:1, :],
                           rhs=bvb[0:1, c * 512:(c + 1) * 512], start=True, stop=True,
                           waits=[t_bvr, k.const_tok, t_tr])
                tbv.append(p.op("vector", "tensor_copy", out=bvb[:, c * 512:(c + 1) * 512],
                                in_=pq[c][:], waits=[tmb]))
            t_bvb = tbv[-1]
            wc = [E3(nc.sbuf_tensor(k.u("a_wc"), [128, 8, 256], BF16)) for _ in range(2)]
            chw = [k.chan("a_w0"), k.chan("a_w1")]
            wfree = {}
            bank_evs = []
            state = {"gi": 0}

            def bank_wait():
                return [bank_evs[-2]] if len(bank_evs) >= 2 else []

            def proj_T(w_d, dst, bias_off, tok_bias):
                for hc in range(4):
                    gi = state["gi"]
                    bi = gi % 2
                    wl = [wfree[gi - 2]] if gi - 2 in wfree else [t_tr]
                    tw = p.dma("gpsimd", wc[bi][:],
                               w_d[:, hc * 256:(hc + 1) * 256].rearrange("(a p) h -> p a h", p=128),
                               chw[bi], waits=wl)
                    for hl in range(2):
                        hp = hc * 2 + hl
                        for qc in range(4):
                            pb = pq[len(bank_evs) % 2]
                            w0 = [tw, t_tr, tok_bias, t_bvb] + bank_wait()
                            for ec in range(8):
                                p.op("tensor", "matmul", out=pb[:, 0:512],
                                     lhsT=wc[bi][:, ec, hl * 128:(hl + 1) * 128],
                                     rhs=xT[:, ec, qc * 512:(qc + 1) * 512],
                                     start=(ec == 0), stop=(ec == 7),
                                     waits=w0 if ec == 0 else (), inc=(ec == 7))
                            tmm = p.tok("tensor")
                            ev = p.op("scalar", "activation",
                                      out=dst[:, hp, qc * 512:(qc + 1) * 512],
                                      in_=pb[:, 0:512], func=AF.Identity,
                                      bias=bqc[:, bias_off + hp:bias_off + hp + 1], scale=1.0,
                                      waits=[tmm])
                            bank_evs.append(ev)
                            wfree[gi] = ev
                    state["gi"] += 1

            proj_T(wqT_d, QT, 0, t_bk)
            proj_T(wkT_d, KT, 8, t_bk)
            for vc in range(4):
                gi = state["gi"]
                bi = gi % 2
                wl = [wfree[gi - 2]] if gi - 2 in wfree else [t_tr]
                tw = p.dma("gpsimd", wc[bi][:],
                           wvT_d[:, vc * 256:(vc + 1) * 256].rearrange("(a p) h -> p a h", p=128),
                           chw[bi], waits=wl)
                for tp in range(NTP):
                    pb = pq[len(bank_evs) % 2]
                    w0 = [tw, t_tr, t_bvb] + bank_wait()
                    for ec in range(8):
                        p.op("tensor", "matmul", out=pb[:, 0:256],
                             lhsT=xT[:, ec, tp * 128:(tp + 1) * 128],
                             rhs=wc[bi][:, ec, 0:256], start=(ec == 0), stop=(ec == 7),
                             waits=w0 if ec == 0 else (), inc=(ec == 7))
                    tmm = p.tok("tensor")
                    ev = p.op("vector", "tensor_tensor",
                              out=V[:, tp, vc * 256:(vc + 1) * 256], in0=pb[:, 0:256],
                              in1=bvb[:, vc * 256:(vc + 1) * 256], op=ALU.add, waits=[tmm])
                    bank_evs.append(ev)
                    wfree[gi] = ev
                state["gi"] += 1
        k.join()
        with ExitStack() as es3:
            E3 = es3.enter_context
            pS_l = [E3(nc.psum_tensor(k.u("a_pS"), [128, 2048], F32)) for _ in range(2)]
            esc = E3(nc.sbuf_tensor(k.u("a_esc"), [128, 1024], BF16))
            stt = E3(nc.sbuf_tensor(k.u("a_stt"), [128, 16], F32))
            dmsk = E3(nc.sbuf_tensor(k.u("a_dmsk"), [128, 128], F32))
            t_res = None
            res_hist = []
            for kt in range(NTP):
                pS = pS_l[kt % 2]
                w0 = [res_hist[-2]] if len(res_hist) >= 2 else []
                for half in range(2):
                    for fc in range(2):
                        o = half * 1024 + fc * 512
                        for ec in range(8):
                            p.op("tensor", "matmul", out=pS[:, o:o + 512],
                                 lhsT=KT[:, ec, kt * 128:(kt + 1) * 128],
                                 rhs=QT[:, ec, o:o + 512],
                                 start=(ec == 0), stop=(ec == 7),
                                 waits=w0 if (half == 0 and fc == 0 and ec == 0) else (),
                                 inc=(ec == 7 and fc == 1 and half == 1))
                t_s = p.tok("tensor")
                p.op("vector", "reduce_max", out=stt[:, 0:1].unsqueeze(2),
                     in_=pS[:, 0:1024].unsqueeze(1), axis=AX.X, waits=[t_s])
                p.op("vector", "reduce_max", out=stt[:, 1:2].unsqueeze(2),
                     in_=pS[:, 1024:2048].unsqueeze(1), axis=AX.X)
                p.op("vector", "tensor_max", out=stt[:, 2:3], in0=stt[:, 0:1], in1=stt[:, 1:2])
                p.op("vector", "tensor_scalar", out=stt[:, 3:4], in0=stt[:, 2:3],
                     scalar1=-1.0 / 64.0, scalar2=None, op0=ALU.mult)
                off = kt * 128
                p.op("vector", "tensor_tensor", out=dmsk[:], in0=pS[:, off:off + 128],
                     in1=k.sb["identity128"][:], op=ALU.mult)
                tdg = p.op("vector", "reduce_sum", out=stt[:, 4:5].unsqueeze(2),
                           in_=dmsk[:].unsqueeze(1), axis=AX.X)
                t1 = p.op("scalar", "activation", out=esc[:, 0:1024], in_=pS[:, 0:1024],
                          func=AF.Exp, scale=1.0 / 64.0, bias=stt[:, 3:4],
                          accum_out=stt[:, 5:6], waits=[tdg])
                t2 = p.op("scalar", "activation", out=esc[:, 0:1024], in_=pS[:, 1024:2048],
                          func=AF.Exp, scale=1.0 / 64.0, bias=stt[:, 3:4],
                          accum_out=stt[:, 6:7])
                t3 = p.op("scalar", "activation", out=stt[:, 7:8], in_=stt[:, 4:5],
                          func=AF.Exp, scale=1.0 / 64.0, bias=stt[:, 3:4])
                p.op("vector", "tensor_add", out=stt[:, 8:9], in0=stt[:, 5:6], in1=stt[:, 6:7],
                     waits=[t2, t3])
                p.op("vector", "reciprocal", out=stt[:, 9:10], in_=stt[:, 8:9])
                p.op("vector", "tensor_mul", out=stt[:, 10:11], in0=stt[:, 7:8], in1=stt[:, 9:10])
                t_res = p.op("vector", "scalar_tensor_tensor", out=res_sb[:, kt, :],
                             in0=V[:, kt, :], scalar=stt[:, 10:11], in1=res_sb[:, kt, :],
                             op0=ALU.mult, op1=ALU.add)
                res_hist.append(t_res)
    return t_res


def emit_moe(k: K, res_sb, routerT_d, w1b_d, w2b_d):
    p = k.p
    nc = k.nc
    k.join()
    with ExitStack() as es2:
        E = es2.enter_context
        xT = None
        w_sb = E(nc.sbuf_tensor(k.u("m_w"), [128, NTP, 8], F32))
        xT, t_tr = emit_ln_transpose(k, res_sb, es2, ())
        with ExitStack() as es3:
            E3 = es3.enter_context
            rw = E3(nc.sbuf_tensor(k.u("m_rw"), [128, 8, 8], BF16))
            plog = E3(nc.sbuf_tensor(k.u("m_plog"), [128, 8], F32))
            stt = E3(nc.sbuf_tensor(k.u("m_stt"), [128, 4], F32))
            pr_l = [E3(nc.psum_tensor(k.u("m_pr"), [128, 512], F32)) for _ in range(2)]
            chr_ = k.chan("m_r")
            trw = p.dma("gpsimd", rw[:], routerT_d.rearrange("(a p) e -> p a e", p=128), chr_,
                        waits=[t_tr])
            ev_hist = []
            for tp in range(NTP):
                pr = pr_l[tp % 2]
                w0 = [trw, t_tr] + ([ev_hist[-2]] if len(ev_hist) >= 2 else [])
                for ec in range(8):
                    p.op("tensor", "matmul", out=pr[:, 0:8],
                         lhsT=xT[:, ec, tp * 128:(tp + 1) * 128], rhs=rw[:, ec, :],
                         start=(ec == 0), stop=(ec == 7),
                         waits=w0 if ec == 0 else (), inc=(ec == 7))
                tmm = p.tok("tensor")
                p.op("vector", "reduce_max", out=stt[:, 0:1].unsqueeze(2),
                     in_=pr[:, 0:8].unsqueeze(1), axis=AX.X, waits=[tmm])
                tneg = p.op("vector", "tensor_scalar", out=stt[:, 1:2], in0=stt[:, 0:1],
                            scalar1=-1.0, scalar2=None, op0=ALU.mult)
                texp = p.op("scalar", "activation", out=plog[:], in_=pr[:, 0:8],
                            func=AF.Exp, scale=1.0, bias=stt[:, 1:2], waits=[tneg])
                p.op("vector", "reduce_sum", out=stt[:, 2:3].unsqueeze(2),
                     in_=plog[:].unsqueeze(1), axis=AX.X, waits=[texp])
                p.op("vector", "reciprocal", out=stt[:, 3:4], in_=stt[:, 2:3])
                p.op("vector", "tensor_scalar", out=plog[:], in0=plog[:],
                     scalar1=stt[:, 3:4], scalar2=None, op0=ALU.mult)
                p.op("vector", "reduce_max", out=stt[:, 0:1].unsqueeze(2),
                     in_=plog[:].unsqueeze(1), axis=AX.X)
                p.op("vector", "scalar_tensor_tensor", out=w_sb[:, tp, :], in0=plog[:],
                     scalar=stt[:, 0:1], in1=plog[:], op0=ALU.is_lt, op1=ALU.mult)
                p.op("vector", "reduce_max", out=stt[:, 1:2].unsqueeze(2),
                     in_=w_sb[:, tp, :].unsqueeze(1), axis=AX.X)
                p.op("vector", "scalar_tensor_tensor", out=w_sb[:, tp, :], in0=plog[:],
                     scalar=stt[:, 1:2], in1=plog[:], op0=ALU.is_ge, op1=ALU.mult)
                ev_hist.append(p.op("vector", "tensor_scalar", out=w_sb[:, tp, :],
                               in0=w_sb[:, tp, :], scalar1=0.5, scalar2=None, op0=ALU.mult))
            t_router = ev_hist[-1]
        with ExitStack() as es3:
            E3 = es3.enter_context
            hT = E3(nc.sbuf_tensor(k.u("m_hT"), [128, 32, 1024], BF16))
            w1c = [E3(nc.sbuf_tensor(k.u("m_w1"), [128, 8, 512], BF16)) for _ in range(2)]
            w2c = [E3(nc.sbuf_tensor(k.u("m_w2"), [128, 4, 512], BF16)) for _ in range(2)]
            pall = E3(nc.psum_tensor(k.u("m_py"), [128, 4096], F32))
            ch1 = [k.chan("m_c10"), k.chan("m_c11")]
            ch2 = [k.chan("m_c20"), k.chan("m_c21")]
            w1_free = {}
            w2_free = {}
            gelu_toks = {}
            yev = {}
            for e in range(NE_MOE):
                for tkb in range(2):
                    li = e * 2 + tkb
                    for fc in range(8):
                        bi = fc % 2
                        key = (li, fc)
                        pkey = (li, fc - 2) if fc >= 2 else (li - 1, 6 + fc)
                        wl = [w1_free[pkey]] if pkey in w1_free else [t_tr, t_router]
                        t1 = p.dma("sync", w1c[bi][:],
                                   w1b_d[e, :, fc * 512:(fc + 1) * 512].rearrange("(a p) f -> p a f", p=128),
                                   ch1[bi], waits=wl)
                        for fpl in range(4):
                            fpt = fc * 4 + fpl
                            for tc in range(2):
                                hidx = fpt * 2 + tc
                                bank = hidx % 8
                                po = pall[:, bank * 512:(bank + 1) * 512]
                                w0 = [t1, t_tr]
                                if (li, hidx - 8) in gelu_toks:
                                    w0.append(gelu_toks[(li, hidx - 8)])
                                elif li > 0 and (li - 1, hidx + 56) in gelu_toks:
                                    w0.append(gelu_toks[(li - 1, hidx + 56)])
                                if li > 0 and len(yev) > 0:
                                    w0.append(yev[max(yev)])
                                for ec in range(8):
                                    p.op("tensor", "matmul", out=po,
                                         lhsT=w1c[bi][:, ec, fpl * 128:(fpl + 1) * 128],
                                         rhs=xT[:, ec, tkb * 1024 + tc * 512: tkb * 1024 + (tc + 1) * 512],
                                         start=(ec == 0), stop=(ec == 7),
                                         waits=w0 if ec == 0 else (),
                                         inc=(ec == 7))
                                tmm = p.tok("tensor")
                                gelu_toks[(li, hidx)] = p.op(
                                    "scalar", "activation",
                                    out=hT[:, fpt, tc * 512:(tc + 1) * 512],
                                    in_=po, func=AF.Gelu, waits=[tmm])
                        w1_free[key] = gelu_toks[(li, (fc * 4 + 3) * 2 + 1)]
                    for dh in range(2):
                        for fc2 in range(8):
                            bi = fc2 % 2
                            key2 = (li, dh, fc2)
                            pk2 = (li, dh, fc2 - 2) if fc2 >= 2 else \
                                  ((li, dh - 1, 6 + fc2) if dh > 0 else
                                   ((li - 1, 1, 6 + fc2) if li > 0 else None))
                            wl = [w2_free[pk2]] if pk2 in w2_free else [t_tr, t_router]
                            t2c = p.dma("sync", w2c[bi][:],
                                        w2b_d[e, fc2 * 512:(fc2 + 1) * 512, dh * 512:(dh + 1) * 512]
                                        .rearrange("(a p) f -> p a f", p=128),
                                        ch2[bi], waits=wl)
                            for tpl in range(8):
                                tp = tkb * 8 + tpl
                                py = pall[:, tpl * 512:(tpl + 1) * 512]
                                w0 = [t2c, gelu_toks[(li, 63)]]
                                prev_ev = yev.get((li, dh - 1, tpl)) if dh > 0 else \
                                    (yev.get((li - 1, 1, tpl)) if li > 0 else None)
                                if fc2 == 0 and prev_ev:
                                    w0.append(prev_ev)
                                for cc in range(4):
                                    p.op("tensor", "matmul", out=py,
                                         lhsT=hT[:, fc2 * 4 + cc, tpl * 128:(tpl + 1) * 128],
                                         rhs=w2c[bi][:, cc, :],
                                         start=(fc2 == 0 and cc == 0), stop=(fc2 == 7 and cc == 3),
                                         waits=w0 if cc == 0 else (),
                                         inc=(cc == 3 and (tpl == 7 or fc2 == 7)))
                                if fc2 == 7:
                                    tmm = p.tok("tensor")
                                    tp_g = tkb * 8 + tpl
                                    yev[(li, dh, tpl)] = p.op(
                                        "vector", "scalar_tensor_tensor",
                                        out=res_sb[:, tp_g, dh * 512:(dh + 1) * 512],
                                        in0=py, scalar=w_sb[:, tp_g, e:e + 1],
                                        in1=res_sb[:, tp_g, dh * 512:(dh + 1) * 512],
                                        op0=ALU.mult, op1=ALU.add, waits=[tmm])
                            w2_free[key2] = p.tok("tensor")
            last = yev[max(yev)]
    return last


def emit_l1attn(k: K, res_sb, wqkvT_d, wprojT_d, bproj_d, f8d_d, d8_sb, dm15_sb, bias_ready):
    p = k.p
    nc = k.nc
    k.join()
    with ExitStack() as es2:
        E = es2.enter_context
        QT = E(nc.sbuf_tensor(k.u("l_QT"), [128, NDP, T], BF16))
        KT = E(nc.sbuf_tensor(k.u("l_KT"), [128, NDP, T], BF16))
        V1 = E(nc.sbuf_tensor(k.u("l_V1"), [128, NTP, NH * 65], BF16))
        with ExitStack() as es3:
            E3 = es3.enter_context
            xT, t_tr = emit_ln_transpose(k, res_sb, es3, ())
            pq = [E3(nc.psum_tensor(k.u("l_pq"), [128, 512], F32)) for _ in range(2)]
            wc = [E3(nc.sbuf_tensor(k.u("l_wc"), [128, 8, 256], BF16)) for _ in range(2)]
            chw = [k.chan("l_w0"), k.chan("l_w1")]
            t_ones = p.op("vector", "memset",
                          sap(V1, 64, [[NTP * NH * 65, 128], [NH * 65, NTP], [65, NH], [1, 1]]),
                          1.0)
            wfree = {}
            bank_evs = []
            state = {"gi": 0}

            def bank_wait():
                return [bank_evs[-2]] if len(bank_evs) >= 2 else []

            def proj_T(col0, dst):
                for hc in range(4):
                    gi = state["gi"]
                    bi = gi % 2
                    wl = [wfree[gi - 2]] if gi - 2 in wfree else [t_tr]
                    tw = p.dma("gpsimd", wc[bi][:],
                               wqkvT_d[:, col0 + hc * 256:col0 + (hc + 1) * 256]
                               .rearrange("(a p) h -> p a h", p=128), chw[bi], waits=wl)
                    for hl in range(2):
                        hp = hc * 2 + hl
                        for qc in range(4):
                            pb = pq[len(bank_evs) % 2]
                            w0 = [tw, t_tr] + bank_wait()
                            for ec in range(8):
                                p.op("tensor", "matmul", out=pb[:, 0:512],
                                     lhsT=wc[bi][:, ec, hl * 128:(hl + 1) * 128],
                                     rhs=xT[:, ec, qc * 512:(qc + 1) * 512],
                                     start=(ec == 0), stop=(ec == 7),
                                     waits=w0 if ec == 0 else (), inc=(ec == 7))
                            tmm = p.tok("tensor")
                            ev = p.op("scalar", "copy",
                                      out=dst[:, hp, qc * 512:(qc + 1) * 512],
                                      in_=pb[:, 0:512], waits=[tmm])
                            bank_evs.append(ev)
                            wfree[gi] = ev
                    state["gi"] += 1

            proj_T(0, QT)
            proj_T(1024, KT)
            for vc in range(4):
                gi = state["gi"]
                bi = gi % 2
                wl = [wfree[gi - 2]] if gi - 2 in wfree else [t_tr]
                tw = p.dma("gpsimd", wc[bi][:],
                           wqkvT_d[:, 2048 + vc * 256:2048 + (vc + 1) * 256]
                           .rearrange("(a p) h -> p a h", p=128), chw[bi], waits=wl)
                for tp in range(NTP):
                    pb = pq[len(bank_evs) % 2]
                    w0 = [tw, t_tr, t_ones] + bank_wait()
                    for ec in range(8):
                        p.op("tensor", "matmul", out=pb[:, 0:256],
                             lhsT=xT[:, ec, tp * 128:(tp + 1) * 128],
                             rhs=wc[bi][:, ec, 0:256], start=(ec == 0), stop=(ec == 7),
                             waits=w0 if ec == 0 else (), inc=(ec == 7))
                    tmm = p.tok("tensor")
                    ev = None
                    for hl in range(4):
                        h = vc * 4 + hl
                        ev = p.op("vector", "tensor_copy",
                                  out=V1[:, tp, h * 65:h * 65 + 64],
                                  in_=pb[:, hl * 64:(hl + 1) * 64],
                                  waits=[tmm] if hl == 0 else ())
                    bank_evs.append(ev)
                    wfree[gi] = ev
                state["gi"] += 1
        k.join()
        with ExitStack() as es3:
            E3 = es3.enter_context
            AT = E3(nc.sbuf_tensor(k.u("l_AT"), [128, NTP, 512], BF16))
            W2r = E3(nc.sbuf_tensor(k.u("l_W2r"), [128, NH, 383], BF16))
            rzbf = E3(nc.sbuf_tensor(k.u("l_rz"), [128, 512], BF16))
            przs = E3(nc.sbuf_tensor(k.u("l_prz"), [128, 512], F32))
            pS = [E3(nc.psum_tensor(k.u("l_pS"), [128, 1024], F32)) for _ in range(2)]
            po_l = [E3(nc.psum_tensor(k.u("l_po"), [128, 512], F32)) for _ in range(2)]
            prz = E3(nc.psum_tensor(k.u("l_przp"), [128, 512], F32))
            chb = k.chan("l_bw")
            src = bass.AP(f8d_d.tensor, 0, [[1, 128], [512, NH], [1, 383]])
            t_w2r = p.dma("gpsimd", W2r[:], src, chb, waits=[])
            t_at_free = {}
            tail = {"pend": None}
            tail_ev = []

            def flush_tail():
                if tail["pend"] is None:
                    return
                tmo, hh, qq, po = tail["pend"]
                tail["pend"] = None
                hr2 = (hh % 2) * 64
                hpt2 = hh // 2
                tln = p.op("scalar", "activation", out=przs[64:65, :],
                           in_=po[64:65, 0:512], func=AF.Ln, waits=[tmo])
                trz = p.op("scalar", "activation", out=rzbf[64:65, :],
                           in_=przs[64:65, :], func=AF.Exp, scale=-1.0,
                           waits=[tln])
                tbc = p.op("tensor", "matmul", out=prz[0:64, 0:512],
                           lhsT=k.sb["ones128"][64:65, 0:64], rhs=rzbf[64:65, :],
                           start=True, stop=True, waits=[trz], inc=True)
                tcp = p.op("scalar", "activation", out=przs[0:64, :],
                           in_=prz[0:64, 0:512], func=AF.Identity,
                           waits=[tbc] + tail_ev[-1:])
                ev = p.op("vector", "tensor_tensor",
                          out=QT[hr2:hr2 + 64, hpt2, qq * 512:(qq + 1) * 512],
                          in0=po[0:64, 0:512], in1=przs[0:64, :],
                          op=ALU.mult, waits=[tcp])
                tail_ev.append(ev)

            texp_prev_last = None
            for h in range(NH):
                hr = (h % 2) * 64
                hpt = h // 2
                for qb in range(4):
                    it = h * 4 + qb
                    q0 = qb * 512
                    texp = {}
                    for kt in range(NTP):
                        pb = pS[(kt // 2) % 2]
                        hf = (kt % 2) * 512
                        c = kt * 128 - q0
                        w0 = []
                        if (h, qb, kt - 4) in t_at_free:
                            w0.append(t_at_free[(h, qb, kt - 4)])
                        elif kt < 4 and texp_prev_last is not None:
                            w0.append(texp_prev_last)
                        p.op("tensor", "matmul", out=pb[:, hf:hf + 512],
                             lhsT=KT[hr:hr + 64, hpt, kt * 128:(kt + 1) * 128],
                             rhs=QT[hr:hr + 64, hpt, q0:q0 + 512],
                             start=True, stop=True, waits=w0 + [bias_ready, t_w2r], inc=True)
                        tmm = p.tok("tensor")
                        if kt == 1:
                            flush_tail()
                        hbias = None
                        if c <= -256:
                            tb = tmm
                        elif c >= 640:
                            hbias = dm15_sb[:, h:h + 1]
                            tb = tmm
                        else:
                            jstar = c - 127
                            tb = tmm
                            if jstar > 0:
                                tb = p.op("vector", "tensor_scalar",
                                          out=pb[:, hf:hf + jstar], in0=pb[:, hf:hf + jstar],
                                          scalar1=d8_sb[:, h:h + 1], scalar2=None,
                                          op0=ALU.add, waits=[tmm])
                            jlo = max(0, jstar)
                            jhi = min(512, c + 256)
                            if jhi > jlo:
                                w2r_ap = sap(W2r, h * 383 + (c + 255 - jlo),
                                             [[NH * 383, 128], [-1, jhi - jlo]])
                                tb = p.op("vector", "tensor_tensor",
                                          out=pb[:, hf + jlo:hf + jhi],
                                          in0=pb[:, hf + jlo:hf + jhi],
                                          in1=w2r_ap, op=ALU.add, waits=[tmm])
                        texp[kt] = (tb, hbias)
                        if kt % 2 == 1:
                            tbs = [texp[kt - 1][0], texp[kt][0]]
                            hb0, hb1 = texp[kt - 1][1], texp[kt][1]
                            if hb0 is None and hb1 is None:
                                te = p.op("scalar", "activation",
                                          out=AT[:, kt - 1:kt + 1, :],
                                          in_=pb[:].rearrange("p (a b) -> p a b", b=512),
                                          func=AF.Exp, scale=0.125,
                                          bias=k.sb["fconsts"][:, 1:2], waits=tbs)
                            else:
                                te = p.op("scalar", "activation", out=AT[:, kt - 1, :],
                                          in_=pb[:, 0:512], func=AF.Exp, scale=0.125,
                                          bias=hb0 if hb0 is not None else k.sb["fconsts"][:, 1:2],
                                          waits=tbs)
                                te = p.op("scalar", "activation", out=AT[:, kt, :],
                                          in_=pb[:, 512:1024], func=AF.Exp, scale=0.125,
                                          bias=hb1 if hb1 is not None else k.sb["fconsts"][:, 1:2],
                                          waits=())
                            t_at_free[(h, qb, kt - 1)] = te
                            t_at_free[(h, qb, kt)] = te
                            texp[kt] = te
                    texp_prev_last = texp[NTP - 1]
                    po = po_l[it % 2]
                    for kt in range(NTP):
                        w0 = [texp[NTP - 1]]
                        if kt == 0 and len(tail_ev) >= 1 and it >= 2:
                            w0.append(tail_ev[it - 2])
                        for dummy in (0,):
                            p.op("tensor", "matmul", out=po[0:65, 0:512],
                                 lhsT=V1[:, kt, h * 65:(h + 1) * 65], rhs=AT[:, kt, :],
                                 start=(kt == 0), stop=(kt == NTP - 1),
                                 waits=w0 if kt == 0 else (), inc=(kt == NTP - 1))
                    tmo = p.tok("tensor")
                    tail["pend"] = (tmo, h, qb, po)
            flush_tail()
        k.join()
        with ExitStack() as es3:
            E3 = es3.enter_context
            wcp = [E3(nc.sbuf_tensor(k.u("l_wp"), [128, 8, 512], BF16)) for _ in range(2)]
            bpr = E3(nc.sbuf_tensor(k.u("l_bpr"), [1, 1024], BF16))
            pp = [E3(nc.psum_tensor(k.u("l_pp"), [128, 512], F32)) for _ in range(2)]
            chp = k.chan("l_wp")
            tbp = p.dma("gpsimd", bpr[:], bproj_d[None, :], k.chan("l_bp"))
            t_res = None
            tw = {}
            for half in range(2):
                tw[half] = p.dma("gpsimd", wcp[half][:],
                                 wprojT_d[:, half * 512:(half + 1) * 512]
                                 .rearrange("(a p) h -> p a h", p=128), chp,
                                 waits=[])
            evp = {}
            for qpt in range(NTP):
                for half in range(2):
                    pb = pp[half]
                    w0 = [tw[1], tbp]
                    if (qpt - 1, half) in evp:
                        w0.append(evp[(qpt - 1, half)])
                    for cc in range(8):
                        p.op("tensor", "matmul", out=pb[:, 0:512],
                             lhsT=QT[:, cc, qpt * 128:(qpt + 1) * 128],
                             rhs=wcp[half][:, cc, :], start=(cc == 0), stop=False,
                             waits=w0 if cc == 0 else (), inc=False)
                    p.op("tensor", "matmul", out=pb[:, 0:512],
                         lhsT=k.sb["ones128"][0:1, :],
                         rhs=bpr[0:1, half * 512:(half + 1) * 512],
                         start=False, stop=True, inc=True)
                    tmm = p.tok("tensor")
                    evp[(qpt, half)] = p.op("vector", "tensor_tensor",
                                            out=res_sb[:, qpt, half * 512:(half + 1) * 512],
                                            in0=pb[:, 0:512],
                                            in1=res_sb[:, qpt, half * 512:(half + 1) * 512],
                                            op=ALU.add, waits=[tmm])
                    t_res = evp[(qpt, half)]
    return t_res


def emit_l1_bias_setup(k: K, table_d):
    p = k.p
    nc = k.nc
    k.join()
    tab = k.sbuf("tab32", [32, 16], F32)
    d8 = k.sbuf("Delta8", [128, 16], F32)
    dm15 = k.sbuf("DeltaM15", [128, 16], F32)
    f8d = k.dscratch("F8_dram", [16, 512], F32)
    ch = k.chan("bias")
    with ExitStack() as es2:
        pb = es2.enter_context(nc.psum_tensor("bias_pb", [128, 512], F32))
        f8 = es2.enter_context(nc.sbuf_tensor("bias_f8", [16, 512], F32))
        t_tab = p.dma("sync", tab[:], table_d[:], ch)
        t_mm = p.op("tensor", "matmul", out=pb[0:16, 0:512], lhsT=tab[:],
                    rhs=k.sb["bias_onehot"][:], start=True, stop=True,
                    waits=[t_tab, k.const_tok], inc=True)
        t_f8 = p.op("vector", "tensor_copy", out=f8[:], in_=pb[0:16, 0:512], waits=[t_mm])
        t_f8d = p.dma("sync", f8d[:], f8[:], ch, waits=[t_f8])
        t_mm2 = p.op("tensor", "matmul", out=pb[:, 0:16], lhsT=k.sb["auxD8"][:],
                     rhs=tab[:], start=True, stop=True, waits=[t_f8], inc=True)
        t_d8 = p.op("vector", "tensor_copy", out=d8[:], in_=pb[:, 0:16], waits=[t_mm2])
        t_dm = p.op("vector", "tensor_scalar", out=dm15[:], in0=d8[:],
                    scalar1=0.125, scalar2=-SHIFT, op0=ALU.mult, op1=ALU.add)
    return f8d, d8, dm15, t_f8d


SUBS_ALL = ("ffn0", "attn0", "ffn0", "moes", "attn1", "moes")


def build_program(subs=SUBS_ALL, n_b=4):
    nc = bass.Bass("TRN2", target_bir_lowering=False, debug=False)
    with ExitStack() as es:
        k = K(nc, es)
        p = k.p
        x_d = k.din("x", [n_b, T, D], F32)
        xo_d = k.dout("xo", [n_b, T, D], F32)
        need = set(subs)
        emit_consts(k)
        args = {}
        if "attn1" in need:
            tab_d = k.din("L1_attn_rel_bias_table", [32, NH], F32)
            f8d, d8, dm15, t_bias = emit_l1_bias_setup(k, tab_d)
            args["attn1pre"] = (f8d, d8, dm15, t_bias)
        if "ffn0" in need:
            wg_d = k.din("L0_ffn_w_gate", [I_FFN, D], F32)
            wu_d = k.din("L0_ffn_w_up", [I_FFN, D], F32)
            wd_d = k.din("L0_ffn_w_down", [D, I_FFN], F32)
            wgqT = k.dscratch("wgqT", [D, I_FFN], BF16)
            wuqT = k.dscratch("wuqT", [D, I_FFN], BF16)
            wdqT = k.dscratch("wdqT", [I_FFN, D], BF16)
            t_prep = emit_l0ffn_prep(k, wg_d, wu_d, wd_d, wgqT, wuqT, wdqT)
            args["ffn0"] = (wgqT, wuqT, wdqT)
        if "attn0" in need:
            args["attn0"] = (k.din("L0_attn_WqT", [D, D], F32),
                             k.din("L0_attn_WkT", [D, D], F32),
                             k.din("L0_attn_WvT", [D, D], F32),
                             k.din("L0_attn_bqc", [128, 8], F32),
                             k.din("L0_attn_bkc", [128, 8], F32),
                             k.din("L0_attn_bv", [D], F32))
        if "moe" in need or "moes" in need:
            rT_d = k.din("L1_ffn_routerT", [D, NE_MOE], F32)
            w1b = k.dscratch("w1b", [NE_MOE, D, H_MOE], BF16)
            w2b = k.dscratch("w2b", [NE_MOE, H_MOE, D], BF16)
            w1t_in = k.din("L1_ffn_w1T_in", [NE_MOE, D, H_MOE], F32)
            w2t_in = k.din("L1_ffn_w2T_in", [NE_MOE, H_MOE, D], F32)
            emit_moe_cast(k, w1t_in, w2t_in, w1b, w2b,
                          waits=[t_prep] if "ffn0" in need else [])
            args["moe"] = (rT_d, w1b, w2b)
            if "moes" in need:
                xln_sc = k.dscratch("sp_xln", [T, D], BF16)
                giw_sc = k.dscratch("sp_giw", [MAXD_SP * PT_SP, 2], F32)
                yd_sc = k.dscratch("sp_yd", [MAXD_SP * PT_SP, D], BF16)
                args["moes"] = (rT_d, w1b, w2b, xln_sc, giw_sc, yd_sc)
        if "attn1" in need:
            f8d, d8, dm15, t_bias = args.pop("attn1pre")
            args["attn1"] = (k.din("L1_attn_w_qkvT", [D, 3 * D], F32),
                             k.din("L1_attn_w_projT", [D, D], F32),
                             k.din("L1_attn_b_proj", [D], F32),
                             f8d, d8, dm15, t_bias)
        res_sb = k.sbuf("res", [128, NTP, D], F32)
        ch_io = k.chan("io")
        t_store = None
        for b in range(n_b):
            k.join()
            t_in = p.dma("sync", res_sb[:],
                         x_d[b].rearrange("(a p) d -> p a d", p=128), ch_io,
                         waits=[t_store] if t_store else [])
            for eng in ("vector", "scalar", "tensor", "gpsimd"):
                p._wait(eng, t_in)
            for s in subs:
                if s == "ffn0":
                    emit_l0ffn(k, res_sb, *args["ffn0"])
                elif s == "attn0":
                    emit_l0attn(k, res_sb, *args["attn0"])
                elif s == "moe":
                    emit_moe(k, res_sb, *args["moe"])
                elif s == "moes":
                    emit_moe_sparse(k, res_sb, *args["moes"])
                elif s == "attn1":
                    a = args["attn1"]
                    emit_l1attn(k, res_sb, a[0], a[1], a[2], a[3], a[4], a[5],
                                bias_ready=a[6])
            k.join()
            t_store = p.dma("sync", xo_d[b].rearrange("(a p) d -> p a d", p=128),
                            res_sb[:], ch_io)
        p.finish()
        p.emit(nc, es)
        innames = [n for n, t in k.dram.items()]
    return nc, k


def host_inputs_for(k: K, inputs, b_slice):
    m = {}
    m["x"] = np.ascontiguousarray(inputs["L0_attn_X_embed"][b_slice])
    for n, arr in k.aux.items():
        if "c_" + n in k.dram:
            m["c_" + n] = arr
    def ct(a):
        return np.ascontiguousarray(a)
    if "L0_ffn_w_gate" in k.dram:
        m["L0_ffn_w_gate"] = ct(inputs["L0_ffn_w_gate"])
        m["L0_ffn_w_up"] = ct(inputs["L0_ffn_w_up"])
        m["L0_ffn_w_down"] = ct(inputs["L0_ffn_w_down"])
    if "L0_attn_WqT" in k.dram:
        m["L0_attn_WqT"] = ct(inputs["L0_attn_Wq"].T)
        m["L0_attn_WkT"] = ct(inputs["L0_attn_Wk"].T)
        m["L0_attn_WvT"] = ct(inputs["L0_attn_Wv"].T)
        m["L0_attn_bqc"] = ct(np.asarray(inputs["L0_attn_bq"]).reshape(8, 128).T)
        m["L0_attn_bkc"] = ct(np.asarray(inputs["L0_attn_bk"]).reshape(8, 128).T)
        m["L0_attn_bv"] = ct(inputs["L0_attn_bv"])
    if "L1_ffn_routerT" in k.dram:
        m["L1_ffn_routerT"] = ct(inputs["L1_ffn_router_w"].T)
        m["L1_ffn_w1T_in"] = ct(np.transpose(inputs["L1_ffn_w1"], (0, 2, 1)))
        m["L1_ffn_w2T_in"] = ct(np.transpose(inputs["L1_ffn_w2"], (0, 2, 1)))
    if "L1_attn_rel_bias_table" in k.dram:
        m["L1_attn_rel_bias_table"] = ct(inputs["L1_attn_rel_bias_table"])
        m["L1_attn_w_qkvT"] = ct(inputs["L1_attn_w_qkv"].T)
        m["L1_attn_w_projT"] = ct(inputs["L1_attn_w_proj"].T)
        m["L1_attn_b_proj"] = ct(inputs["L1_attn_b_proj"])
    return m


PT_SP = 384
MAXD_SP = 19


def emit_moe_sparse(k: K, res_sb, routerT_d, w1b_d, w2b_d, xln_d, giw_d, yd_d):
    p = k.p
    nc = k.nc
    k.join()
    with ExitStack() as es2:
        E = es2.enter_context
        oh1 = E(nc.sbuf_tensor(k.u("sp_oh1"), [128, NTP, 8], F32))
        oh2 = E(nc.sbuf_tensor(k.u("sp_oh2"), [128, NTP, 8], F32))
        ohb = E(nc.sbuf_tensor(k.u("sp_ohb"), [128, NTP, 8], BF16))
        wt = E(nc.sbuf_tensor(k.u("sp_wt"), [128, NTP, 8], F32))
        ranks = E(nc.sbuf_tensor(k.u("sp_rk"), [128, NTP, 2], F32))
        ranki = E(nc.sbuf_tensor(k.u("sp_rki"), [128, NTP, 2], mybir.dt.int32))
        wj = E(nc.sbuf_tensor(k.u("sp_wj"), [128, NTP, 2], F32))
        px = E(nc.sbuf_tensor(k.u("sp_px"), [128, NTP, 8], F32))
        sm = E(nc.sbuf_tensor(k.u("sp_sm"), [128, 32], F32))
        ebc = E(nc.sbuf_tensor(k.u("sp_ebc"), [128, 64], F32))
        eof = E(nc.sbuf_tensor(k.u("sp_eof"), [1, MAXD_SP], F32))
        with ExitStack() as esr:
            Er = esr.enter_context
            ch_x = (k.chan("sp_x0"), k.chan("sp_x1"))
            xT, t_tr = emit_ln_transpose(k, res_sb, esr, (), store=(xln_d, ch_x))
            t_xst = [(c.key, c.n) for c in ch_x]
            rw = Er(nc.sbuf_tensor(k.u("sp_rw"), [128, 8, 8], BF16))
            lg = Er(nc.sbuf_tensor(k.u("sp_lg"), [128, NTP, 8], F32))
            s16 = Er(nc.sbuf_tensor(k.u("sp_s16"), [128, 64], F32))
            pr_l = [Er(nc.psum_tensor(k.u("sp_pr"), [128, 512], F32)) for _ in range(2)]
            chr_ = k.chan("sp_r")
            trw = p.dma("gpsimd", rw[:], routerT_d.rearrange("(a p) e -> p a e", p=128),
                        chr_, waits=[t_tr] + list(t_xst))
            ev_hist = []
            for tp in range(NTP):
                pr = pr_l[tp % 2]
                w0 = [trw, t_tr] + ([ev_hist[-2]] if len(ev_hist) >= 2 else [])
                for ec in range(8):
                    p.op("tensor", "matmul", out=pr[:, 0:8],
                         lhsT=xT[:, ec, tp * 128:(tp + 1) * 128], rhs=rw[:, ec, :],
                         start=(ec == 0), stop=(ec == 7),
                         waits=w0 if ec == 0 else (), inc=(ec == 7))
                tmm = p.tok("tensor")
                ev_hist.append(p.op("vector", "tensor_copy", out=lg[:, tp, :],
                                    in_=pr[:, 0:8], waits=[tmm]))

            def b16(col):
                return sap(s16, col, [[64, 128], [1, NTP], [0, 8]])

            p.op("vector", "tensor_reduce", out=s16[:, 0:NTP].unsqueeze(2),
                 in_=lg[:], axis=AX.X, op=ALU.max)
            p.op("vector", "tensor_tensor", out=lg[:], in0=lg[:], in1=b16(0),
                 op=ALU.subtract)
            p.op("scalar", "activation", out=lg[:], in_=lg[:], func=AF.Exp,
                 waits=[p.tok("vector")])
            p.op("vector", "tensor_reduce", out=s16[:, 16:16 + NTP].unsqueeze(2),
                 in_=lg[:], axis=AX.X, op=ALU.add, waits=[p.tok("scalar")])
            p.op("vector", "reciprocal", out=s16[:, 16:16 + NTP], in_=s16[:, 16:16 + NTP])
            p.op("vector", "tensor_tensor", out=lg[:], in0=lg[:], in1=b16(16),
                 op=ALU.mult)
            p.op("vector", "tensor_reduce", out=s16[:, 32:32 + NTP].unsqueeze(2),
                 in_=lg[:], axis=AX.X, op=ALU.max)
            p.op("vector", "tensor_tensor", out=oh1[:], in0=lg[:], in1=b16(32),
                 op=ALU.is_ge)
            p.op("vector", "tensor_tensor", out=wt[:], in0=lg[:], in1=b16(32),
                 op=ALU.is_lt)
            p.op("vector", "tensor_mul", out=wt[:], in0=wt[:], in1=lg[:])
            p.op("vector", "tensor_reduce", out=s16[:, 48:48 + NTP].unsqueeze(2),
                 in_=wt[:], axis=AX.X, op=ALU.max)
            p.op("vector", "tensor_tensor", out=oh2[:], in0=lg[:], in1=b16(48),
                 op=ALU.is_ge)
            p.op("vector", "tensor_copy", out=ohb[:], in_=oh2[:])
            p.op("vector", "tensor_sub", out=oh2[:], in0=oh2[:], in1=oh1[:])
            t_router = p.op("vector", "tensor_scalar", out=wt[:], in0=lg[:],
                            scalar1=0.5, scalar2=None, op0=ALU.mult)
        k.join()
        with ExitStack() as es3:
            E3 = es3.enter_context
            pc = E3(nc.psum_tensor(k.u("sp_pc"), [128, 512], F32))
            ebrow = E3(nc.sbuf_tensor(k.u("sp_ebr"), [1, 64], BF16))
            for tp in range(NTP):
                p.op("tensor", "matmul", out=pc[0:1, 0:8],
                     lhsT=k.sb["ones128"][:, 0:1], rhs=ohb[:, tp, :],
                     start=(tp == 0), stop=(tp == NTP - 1),
                     waits=[t_router] if tp == 0 else (), inc=(tp == NTP - 1))
            tcnt = p.tok("tensor")
            p.op("vector", "tensor_copy", out=sm[0:1, 0:8], in_=pc[0:1, 0:8], waits=[tcnt])
            p.op("vector", "tensor_scalar", out=sm[0:1, 8:16], in0=sm[0:1, 0:8],
                 scalar1=0.0, scalar2=None, op0=ALU.is_gt)
            for kk in range(1, 11):
                p.op("vector", "scalar_tensor_tensor", out=sm[0:1, 8:16],
                     in0=sm[0:1, 0:8], scalar=float(PT_SP * kk), in1=sm[0:1, 8:16],
                     op0=ALU.is_gt, op1=ALU.add)
            p.op("vector", "tensor_scalar", out=sm[0:1, 8:16], in0=sm[0:1, 8:16],
                 scalar1=float(PT_SP), scalar2=None, op0=ALU.mult)
            p.op("vector", "memset", sm[0:1, 16:17], 0.0)
            for e in range(1, 9):
                p.op("vector", "tensor_add", out=sm[0:1, 16 + e:17 + e],
                     in0=sm[0:1, 15 + e:16 + e], in1=sm[0:1, 7 + e:8 + e])
            p.op("vector", "memset", eof[:], 0.0)
            for e in range(1, 9):
                p.op("vector", "scalar_tensor_tensor", out=eof[:], in0=k.sb["dtpos"][:],
                     scalar=sm[0:1, 16 + e:17 + e], in1=eof[:],
                     op0=ALU.is_ge, op1=ALU.add)
            p.op("vector", "tensor_scalar", out=eof[:], in0=eof[:], scalar1=7.0,
                 scalar2=None, op0=ALU.min)
            p.op("vector", "tensor_scalar", out=ebrow[0:1, 0:19], in0=eof[:],
                 scalar1=256.0, scalar2=None, op0=ALU.mult)
            p.op("vector", "tensor_scalar", out=ebrow[0:1, 19:38], in0=eof[:],
                 scalar1=512.0, scalar2=None, op0=ALU.mult)
            tvb = p.op("vector", "tensor_copy", out=ebrow[0:1, 38:46], in_=sm[0:1, 16:24])
            tbc = p.op("tensor", "matmul", out=pc[:, 0:46], lhsT=k.sb["ones128"][0:1, :],
                       rhs=ebrow[0:1, 0:46], start=True, stop=True, waits=[tvb], inc=True)
            tebc = p.op("vector", "tensor_copy", out=ebc[:, 0:46], in_=pc[:, 0:46], waits=[tbc])
            tpxev = None
            for tpo in range(NTP):
                first = True
                for tpi in range(tpo + 1):
                    lhs = k.sb["sut128"][:] if tpi == tpo else k.sb["ones128"][:]
                    w0 = ([tebc] if tpo == 0 else [tpxev]) if first else ()
                    p.op("tensor", "matmul", out=pc[:, 64:72], lhsT=lhs,
                         rhs=ohb[:, tpi, :], start=first, stop=(tpi == tpo),
                         waits=w0, inc=(tpi == tpo))
                    first = False
                tpx = p.tok("tensor")
                tpxev = p.op("vector", "tensor_copy", out=px[:, tpo, :], in_=pc[:, 64:72],
                             waits=[tpx])
            scr8 = E3(nc.sbuf_tensor(k.u("sp_s8"), [128, 8], F32))
            for tp in range(NTP):
                p.op("vector", "tensor_add", out=px[:, tp, :], in0=px[:, tp, :],
                     in1=ebc[:, 38:46])
                for j, oh in ((0, oh1), (1, oh2)):
                    p.op("vector", "tensor_mul", out=scr8[:], in0=oh[:, tp, :],
                         in1=px[:, tp, :])
                    p.op("vector", "reduce_sum", out=ranks[:, tp, j:j + 1].unsqueeze(2),
                         in_=scr8[:].unsqueeze(1), axis=AX.X)
                    p.op("vector", "tensor_mul", out=scr8[:], in0=oh[:, tp, :],
                         in1=wt[:, tp, :])
                    p.op("vector", "reduce_sum", out=wj[:, tp, j:j + 1].unsqueeze(2),
                         in_=scr8[:].unsqueeze(1), axis=AX.X)
            trk = p.op("vector", "tensor_copy", out=ranki[:], in_=ranks[:])
        ch_sc = k.chan("sp_sc")
        ch_z = k.chan("sp_z")
        with ExitStack() as es3:
            zt = es3.enter_context(nc.sbuf_tensor(k.u("sp_zt"), [128, 114], F32))
            pairs = es3.enter_context(nc.sbuf_tensor(k.u("sp_pr"), [128, NTP, 2, 2], F32))
            tz = p.op("vector", "memset", zt[:], 0.0)
            p.dma("sync", giw_d.rearrange("(p a) x -> p (a x)", p=128),
                  zt[:], ch_z, waits=[tz])
            for j in range(2):
                p.op("vector", "tensor_copy",
                     out=sap(pairs, j * 2, [[NTP * 4, 128], [4, NTP]]),
                     in_=k.sb["tokidf"][:], waits=[trk] if j == 0 else ())
            tpw = p.op("vector", "tensor_copy",
                       out=sap(pairs, 1, [[NTP * 4, 128], [4, NTP], [2, 2]]),
                       in_=wj[:])
            for tp in range(NTP):
                for j in range(2):
                    p.idma(ch_sc, out=giw_d[:, :],
                           out_offset=bass.IndirectOffsetOnAxis(ranki[:, tp, j:j + 1], 0),
                           in_=pairs[:, tp, j, :], in_offset=None,
                           waits=[trk, tpw, (ch_z.key, ch_z.n)])
            t_scat = (ch_sc.key, ch_sc.n)
        with ExitStack() as es3:
            E3 = es3.enter_context
            idxw1 = [E3(nc.sbuf_tensor(k.u("sp_iw1"), [128, 2], mybir.dt.int32))
                     for _ in range(2)]
            idxw2 = [E3(nc.sbuf_tensor(k.u("sp_iw2"), [128, 4], mybir.dt.int32))
                     for _ in range(2)]
            itmp = E3(nc.sbuf_tensor(k.u("sp_it"), [128, 4], F32))
            xg = [E3(nc.sbuf_tensor(k.u("sp_xg"), [128, 1024], BF16)) for _ in range(3)]
            xgT = E3(nc.sbuf_tensor(k.u("sp_xgT"), [128, 8, PT_SP], BF16))
            gix = E3(nc.sbuf_tensor(k.u("sp_gix"), [128, 8], mybir.dt.int32))
            gixf = E3(nc.sbuf_tensor(k.u("sp_gxf"), [128, 2, 3, 2], F32))
            w1f = E3(nc.sbuf_tensor(k.u("sp_w1f"), [128, 8, H_MOE], BF16))
            w2c = [E3(nc.sbuf_tensor(k.u("sp_w2c"), [128, 8, 1024], BF16)) for _ in range(2)]
            hts = E3(nc.sbuf_tensor(k.u("sp_hts"), [128, 32, PT_SP], BF16))
            yst_l = [E3(nc.sbuf_tensor(k.u("sp_yst"), [128, 1024], BF16))
                     for _ in range(2)]
            st_tok = {}
            last_store = {}
            ptf = E3(nc.psum_tensor(k.u("sp_ptf"), [128, 512], F32))
            py = [E3(nc.psum_tensor(k.u("sp_py"), [128, 512], F32)) for _ in range(7)]
            ch_g = [k.chan("sp_g0"), k.chan("sp_g1")]
            ch_gw = k.chan("sp_gw")
            ch_gx = [k.chan(f"sp_gx{i}") for i in range(4)]
            ch_w1 = k.chan("sp_cw1")
            ch_w2 = [k.chan(f"sp_cw2{i}") for i in range(2)]
            ch_y = [k.chan("sp_y0"), k.chan("sp_y1")]
            gfree = {}
            xg_last = {}
            gelu_t = {}
            yst_free = {}
            w2_used = {}
            t_h_done = {}
            t_w1g = {}
            t_idx = {}

            def emit_idx(dt, wait_extra):
                b = dt % 2
                w0 = [tebc] + wait_extra
                p.op("vector", "tensor_scalar", out=itmp[:, 0:2], in0=k.sb["rcw1"][:],
                     scalar1=ebc[:, dt:dt + 1], scalar2=None, op0=ALU.add, waits=w0)
                p.op("vector", "tensor_copy", out=idxw1[b][:], in_=itmp[:, 0:2])
                p.op("vector", "tensor_scalar", out=itmp[:, 0:4], in0=k.sb["rcw2"][:],
                     scalar1=ebc[:, 19 + dt:20 + dt], scalar2=None, op0=ALU.add)
                return p.op("vector", "tensor_copy", out=idxw2[b][:], in_=itmp[:, 0:4])

            ctk = list(getattr(k, "cast_toks", []))

            def emit_w1_gather(dt, waits):
                b = dt % 2
                t = None
                for half in range(2):
                    t = p.idma(ch_w1,
                               out=w1f[:, half * 4:(half + 1) * 4, :]
                               .rearrange("p a b -> p (a b)"),
                               out_offset=None,
                               in_=w1b_d.rearrange("e (g a) f -> (e g) (a f)", a=4),
                               in_offset=bass.IndirectOffsetOnAxis(
                                   idxw1[b][:, half:half + 1], 0),
                               waits=(list(waits) + ctk) if half == 0 else [])
                return t

            t_idx[0] = emit_idx(0, [(ch_z.key, ch_z.n), t_scat])
            t_w1g[0] = emit_w1_gather(0, [t_idx[0]])
            NPT_D = PT_SP // 128

            tx_l = {}

            def emit_xgather_issue(dt, wextra=()):
                bb = dt % 2
                b = bb * 4
                wgi = [t_scat] + list(wextra)
                if dt > 1:
                    wgi += [(c.key, c.n) for c in ch_gx]
                tgi = None
                for pt2 in range(NPT_D):
                    tgi = p.dma("sync", gixf[:, bb, pt2, :],
                                giw_d[dt * PT_SP + pt2 * 128:dt * PT_SP + (pt2 + 1) * 128, :],
                                ch_g[bb], waits=wgi if pt2 == 0 else [])
                tcv = p.op("vector", "tensor_copy",
                           out=gix[:, b:b + NPT_D],
                           in_=sap(gixf, bb * 6, [[12, 128], [2, 3]]),
                           waits=[tgi])
                for pt in range(NPT_D):
                    w0 = [tcv] + list(t_xst)
                    if (dt - 1, pt) in gfree:
                        w0.append(gfree[(dt - 1, pt)])
                    tx_l[(dt, pt)] = p.idma(
                        ch_gx[pt], out=xg[pt][:], out_offset=None,
                        in_=xln_d[:, :],
                        in_offset=bass.IndirectOffsetOnAxis(gix[:, b + pt:b + pt + 1], 0),
                        waits=w0)

            def emit_xgather_tr(dt):
                tev = None
                for pt in range(NPT_D):
                    tx = tx_l[(dt, pt)]
                    tg30 = gelu_t.get((dt - 1, 30))
                    for half in range(2):
                        for dpl in range(4):
                            dp = half * 4 + dpl
                            w = [tx, k.const_tok]
                            if tev is not None:
                                w.append(tev)
                            elif (dt - 1, NPT_D - 1) in gfree:
                                w.append(gfree[(dt - 1, NPT_D - 1)])
                            if pt == 0 and half == 0 and dpl == 0 and tg30:
                                w.append(tg30)
                            p.op("tensor", "matmul",
                                 out=py[6][:, dpl * 128:(dpl + 1) * 128],
                                 lhsT=sap(xg[pt], dp, [[1024, 128], [8, 128]]),
                                 rhs=k.sb["identity128"][:],
                                 start=True, stop=True,
                                 waits=w if dpl == 0 else (), inc=(dpl == 3))
                        mt = p.tok("tensor")
                        tev = p.op("vector", "tensor_copy",
                                   out=xgT[:, half * 4:(half + 1) * 4,
                                           pt * 128:(pt + 1) * 128],
                                   in_=py[6][:].rearrange("p (a b) -> p a b", b=128),
                                   waits=[mt])
                    gfree[(dt, pt)] = tev
                return tev

            emit_xgather_issue(0)
            t_xgT = {0: emit_xgather_tr(0)}
            emit_xgather_issue(1)
            for dt in range(MAXD_SP):
                for fc in range(8):
                    for fpl in range(4):
                        fpt = fc * 4 + fpl
                        fpt8, jh = fpt // 8, fpt % 8
                        bank = py[6] if fpt % 2 == 0 else ptf
                        w0 = [t_w1g[dt], t_xgT[dt]]
                        if (dt, fpt - 2) in gelu_t:
                            w0.append(gelu_t[(dt, fpt - 2)])
                        elif dt > 0 and (dt - 1, fpt + 30) in gelu_t:
                            w0.append(gelu_t[(dt - 1, fpt + 30)])
                        for ec in range(8):
                            p.op("tensor", "matmul", out=bank[:, 0:PT_SP],
                                 lhsT=sap(w1f, ec * H_MOE + fpt8 * 1024 + jh,
                                          [[8 * H_MOE, 128], [8, 128]]),
                                 rhs=xgT[:, ec, :], start=(ec == 0), stop=(ec == 7),
                                 waits=w0 if ec == 0 else (), inc=(ec == 7))
                        tmm = p.tok("tensor")
                        gw0 = [tmm]
                        if fpt == 0 and dt > 0 and (dt - 1, 3) in w2_used:
                            gw0.append(w2_used[(dt - 1, 3)])
                        gelu_t[(dt, fpt)] = p.op("scalar", "activation",
                                                 out=hts[:, fpt, :], in_=bank[:, 0:PT_SP],
                                                 func=AF.Gelu, waits=gw0)
                t_h_done[dt] = p.tok("tensor")
                if dt + 1 < MAXD_SP:
                    t_xgT[dt + 1] = emit_xgather_tr(dt + 1)

                if dt + 1 < MAXD_SP:
                    t_idx[dt + 1] = emit_idx(dt + 1, [(ch_w1.key, ch_w1.n)] +
                                             [(c.key, c.n) for c in ch_w2])
                for fpt8 in range(4):
                    bi = fpt8 % 2
                    wl = [t_idx[dt]] + ctk
                    if (dt, fpt8 - 2) in w2_used:
                        wl.append(w2_used[(dt, fpt8 - 2)])
                    elif dt > 0 and (dt - 1, fpt8 + 2) in w2_used:
                        wl.append(w2_used[(dt - 1, fpt8 + 2)])
                    t2c = p.idma(ch_w2[bi],
                                 out=w2c[bi][:].rearrange("p a b -> p (a b)"),
                                 out_offset=None,
                                 in_=w2b_d.rearrange("e (g a) f -> (e g) (a f)", a=8),
                                 in_offset=bass.IndirectOffsetOnAxis(
                                     idxw2[dt % 2][:, fpt8:fpt8 + 1], 0),
                                 waits=wl)
                    if dt + 1 < MAXD_SP and fpt8 in (1, 2):
                        half = fpt8 - 1
                        t_w1g[dt + 1] = p.idma(
                            ch_w1,
                            out=w1f[:, half * 4:(half + 1) * 4, :]
                            .rearrange("p a b -> p (a b)"),
                            out_offset=None,
                            in_=w1b_d.rearrange("e (g a) f -> (e g) (a f)", a=4),
                            in_offset=bass.IndirectOffsetOnAxis(
                                idxw1[(dt + 1) % 2][:, half:half + 1], 0),
                            waits=([t_idx[dt + 1], t_h_done[dt]] + ctk)
                            if half == 0 else [])
                    for jh in range(8):
                        fpt = fpt8 * 8 + jh
                        for tpl in range(3):
                            for dh in range(2):
                                ti_ = tpl * 2 + dh
                                pyb = py[ti_]
                                w0 = [gelu_t[(dt, fpt)]]
                                if jh == 0:
                                    w0.append(t2c)
                                if fpt == 0 and ti_ == 0 and dt > 0:
                                    w0 += [yst_free[(dt - 1, t)] for t in range(3)]
                                p.op("tensor", "matmul", out=pyb[:, 0:512],
                                     lhsT=hts[:, fpt, tpl * 128:(tpl + 1) * 128],
                                     rhs=w2c[bi][:, jh, dh * 512:(dh + 1) * 512],
                                     start=(fpt == 0), stop=(fpt == 31),
                                     waits=w0 if ti_ == 0 else (),
                                     inc=(fpt == 31 or ti_ == 5))
                    w2_used[(dt, fpt8)] = p.tok("tensor")
                for tpl in range(3):
                    yb = yst_l[tpl % 2]
                    tev_y = None
                    for dh in range(2):
                        ti_ = tpl * 2 + dh
                        wv = [w2_used[(dt, 3)], tx_l[(dt, NPT_D - 1)]]
                        if dh == 0:
                            if tpl == 2:
                                wv.append(st_tok[(dt, 0)])
                            elif dt > 0:
                                wv.append(st_tok[(dt - 1, 2 - tpl)])
                        tev_y = p.op("vector", "tensor_scalar",
                                     out=yb[:, dh * 512:(dh + 1) * 512],
                                     in0=py[ti_][:, 0:512],
                                     scalar1=gixf[:, dt % 2, tpl, 1:2],
                                     scalar2=None, op0=ALU.mult, waits=wv)
                    bch = tpl % 2
                    st_tok[(dt, tpl)] = p.dma(
                        "sync",
                        yd_d[dt * PT_SP + tpl * 128:dt * PT_SP + (tpl + 1) * 128, :],
                        yb[:], ch_y[bch],
                        waits=[tev_y] + ([last_store[bch]] if bch in last_store else []))
                    last_store[bch] = st_tok[(dt, tpl)]
                    yst_free[(dt, tpl)] = st_tok[(dt, tpl)]
                if dt + 2 < MAXD_SP:
                    emit_xgather_issue(dt + 2,
                                       [yst_free[(dt, t)] for t in range(3)])
            t_yd = [(c.key, c.n) for c in ch_y]
        with ExitStack() as es3:
            E3 = es3.enter_context
            g1 = [E3(nc.sbuf_tensor(k.u("sp_g1"), [128, 1024], BF16)) for _ in range(4)]
            g2 = [E3(nc.sbuf_tensor(k.u("sp_g2"), [128, 1024], BF16)) for _ in range(4)]
            ch_f = [[k.chan(f"sp_f{i}0"), k.chan(f"sp_f{i}1")] for i in range(4)]
            tadds = {}
            t_res = None
            for tp in range(NTP):
                bi = tp % 4
                wl = list(t_yd) + [trk] + ([tadds[tp - 4]] if tp - 4 in tadds else [])
                tg1 = p.idma(ch_f[bi][0], out=g1[bi][:], out_offset=None, in_=yd_d[:, :],
                             in_offset=bass.IndirectOffsetOnAxis(ranki[:, tp, 0:1], 0),
                             waits=wl)
                tg2 = p.idma(ch_f[bi][1], out=g2[bi][:], out_offset=None, in_=yd_d[:, :],
                             in_offset=bass.IndirectOffsetOnAxis(ranki[:, tp, 1:2], 0),
                             waits=[])
                p.op("vector", "tensor_tensor", out=res_sb[:, tp, :],
                     in0=res_sb[:, tp, :], in1=g1[bi][:], op=ALU.add, waits=[tg1, tg2])
                t_res = p.op("vector", "tensor_tensor", out=res_sb[:, tp, :],
                             in0=res_sb[:, tp, :], in1=g2[bi][:], op=ALU.add)
                tadds[tp] = t_res
    return t_res



from concourse.bass_utils import run_bass_kernel_spmd

_CACHE = {}


def kernel(**inputs):
    if "prog" not in _CACHE:
        _CACHE["prog"] = build_program(SUBS_ALL, n_b=4)
    nc, k = _CACHE["prog"]
    in_maps = [host_inputs_for(k, inputs, slice(4 * c, 4 * c + 4)) for c in range(8)]
    res = run_bass_kernel_spmd(nc, in_maps, core_ids=list(range(8)))
    out = np.concatenate([np.asarray(r["xo"]) for r in res.results], axis=0)
    return out.astype(np.float32)
